# version 02 (final)
# baseline (speedup 1.0000x reference)
import os
import numpy as np
from contextlib import ExitStack

import concourse.bass as bass
import concourse.mybir as mybir
from concourse.bass_utils import run_bass_kernel_spmd

F32 = mybir.dt.float32
BF16 = mybir.dt.bfloat16
AF = mybir.ActivationFunctionType
ALU = mybir.AluOpType
AX = mybir.AxisListType

N_CORES = 8
N = 4096
D = 1024
NH = 16
DFF = 4096
NE = 8
NTS = N // 128
NEG = -1.0e30
POLY = (-0.037, 0.481, -2.3, 4.917, -1.791)
LN_EPS = 1e-5

ENGINES = ("sync", "tensor", "vector", "scalar", "gpsimd")


class Sem:
    __slots__ = ("h", "n")

    def __init__(self, handle):
        self.h = handle
        self.n = 0


class B:
    def __init__(self, nc, ctx):
        self.nc = nc
        self.ctx = ctx
        self.stream = {e: [] for e in ENGINES}
        self._si = 0

    def sem(self, name=None):
        self._si += 1
        h = self.ctx.enter_context(self.nc.semaphore(name or f"s{self._si}"))
        return Sem(h)

    def wait(self, eng, sem, val=None):
        v = sem.n if val is None else val
        if v <= 0:
            return
        self.stream[eng].append(lambda E, s=sem.h, vv=v: E.wait_ge(s, vv))

    def twait(self, eng, fc):
        if fc is not None:
            self.wait(eng, fc[0], fc[1])

    def do(self, eng, emit, inc=None):
        if inc is not None:
            inc[0].n += inc[1]

        def fn(E, emit=emit, inc=inc):
            ins = emit(E)
            if inc is not None:
                ins.then_inc(inc[0].h, inc[1])
        self.stream[eng].append(fn)

    def dma(self, eng, sem, out_ap, in_ap, **kw):
        self.do(eng, lambda E, o=out_ap, i=in_ap, kw=kw:
                E.dma_start(out=o, in_=i, **kw), inc=(sem, 16))
        return sem.n


def build_program(nc, upto="all", debug_scratch=False):
    ctx = ExitStack()
    scr_kind = "ExternalOutput" if debug_scratch else "Internal"

    x_d = nc.dram_tensor("x", [N, D], F32, kind="ExternalInput").ap()
    wq_d = nc.dram_tensor("wq", [D, D], F32, kind="ExternalInput").ap()
    wk_d = nc.dram_tensor("wk", [D, D], F32, kind="ExternalInput").ap()
    wv_d = nc.dram_tensor("wv", [D, D], F32, kind="ExternalInput").ap()
    wo_d = nc.dram_tensor("wo", [D, D], F32, kind="ExternalInput").ap()
    bo_d = nc.dram_tensor("bo", [1, D], F32, kind="ExternalInput").ap()
    wr_d = nc.dram_tensor("wr", [D, NE], F32, kind="ExternalInput").ap()
    br_d = nc.dram_tensor("br", [NE, 1], F32, kind="ExternalInput").ap()
    w1_d = nc.dram_tensor("w1", [NE, D, DFF], F32, kind="ExternalInput").ap()
    b1_d = nc.dram_tensor("b1", [NE, DFF], F32, kind="ExternalInput").ap()
    w2_d = nc.dram_tensor("w2", [NE, DFF, D], F32, kind="ExternalInput").ap()
    b2_d = nc.dram_tensor("b2", [NE, D], F32, kind="ExternalInput").ap()
    y_d = nc.dram_tensor("y", [N, D], F32, kind="ExternalOutput").ap()

    xfm_d = nc.dram_tensor("xfm_scr", [D, N], BF16, kind=scr_kind).ap()
    r1fm_d = nc.dram_tensor("r1fm_scr", [D, N], BF16, kind=scr_kind).ap()
    r1tm_d = nc.dram_tensor("r1tm_scr", [N, D], BF16, kind=scr_kind).ap()
    mg_d = nc.dram_tensor("mg_scr", [NE, N], BF16, kind=scr_kind).ap()
    acc_d = nc.dram_tensor("acc_scr", [D, N], BF16, kind=scr_kind).ap() \
        if debug_scratch else None
    s2_d = nc.dram_tensor("s2_scr", [N, D], F32, kind=scr_kind).ap() \
        if debug_scratch else None
    atm_d = nc.dram_tensor("atm_scr", [N, D], BF16, kind=scr_kind).ap() \
        if debug_scratch else None

    b = B(nc, ctx)

    def SB(c, name, shape, dt=F32):
        return c.enter_context(nc.sbuf_tensor(name, shape, dt))

    def PS(c, name, shape, dt=F32):
        return c.enter_context(nc.psum_tensor(name, shape, dt))

    identb = SB(ctx, "identb", [128, 128], BF16)
    identf = SB(ctx, "identf", [128, 128], F32)
    c_eps = SB(ctx, "c_eps", [128, 1], F32)
    junk = SB(ctx, "junk", [128, 512], F32)
    stt_ = SB(ctx, "stats", [128, 8], F32)
    c_mhalf = SB(ctx, "c_mhalf", [128, 1], F32)
    nc.const_aps.register(F32, -0.5, c_mhalf[:, :])

    sLD = b.sem("ld")
    sSET = b.sem("set")

    p0c = ExitStack()
    xtmb = SB(p0c, "xtmb", [128, 2, D], BF16)
    xfst = SB(p0c, "xfst", [128, 2, D], BF16)
    pXT = PS(p0c, "pXT", [128, 2, D], BF16)

    g = "gpsimd"
    b.do(g, lambda E: E.memset(identb[:, :], 0.0), inc=(sSET, 1))
    b.do(g, lambda E: E.affine_select(out=identb[:, :], in_=identb[:, :],
                                      compare_op=ALU.not_equal, fill=1.0, base=0,
                                      pattern=[[-1, 128]], channel_multiplier=1),
         inc=(sSET, 1))
    b.do(g, lambda E: E.memset(identf[:, :], 0.0), inc=(sSET, 1))
    b.do(g, lambda E: E.affine_select(out=identf[:, :], in_=identf[:, :],
                                      compare_op=ALU.not_equal, fill=1.0, base=0,
                                      pattern=[[-1, 128]], channel_multiplier=1),
         inc=(sSET, 1))
    b.do(g, lambda E: E.memset(c_eps[:, :], LN_EPS), inc=(sSET, 1))
    b.do(g, lambda E: E.memset(c_mhalf[:, :], -0.5), inc=(sSET, 1))
    nSET = sSET.n

    sXL = [b.sem("p0xl0"), b.sem("p0xl1")]
    sXT = b.sem("p0xt")
    sXE = b.sem("p0xe")
    sXS = [b.sem("p0xs0"), b.sem("p0xs1")]
    b.wait("tensor", sSET, nSET)
    for t in range(NTS):
        r = t % 2
        if t >= 2:
            b.wait(g, sXT, t - 1)
        b.dma(g, sXL[r], xtmb[:, r, :], x_d[t * 128:(t + 1) * 128, :])
        b.wait("tensor", sXL[r], sXL[r].n)
        if t >= 2:
            b.wait("tensor", sXE, t - 1)
        for c in range(8):
            b.do("tensor", lambda E, r=r, c=c: E.transpose(
                pXT[:, r, c * 128:(c + 1) * 128], xtmb[:, r, c * 128:(c + 1) * 128],
                identb[:, :]), inc=(sXT, 1) if c == 7 else None)
        b.wait("vector", sXT, t + 1)
        if t >= 2:
            b.wait("vector", sXS[r], sXS[r].n)
        b.do("vector", lambda E, r=r: E.tensor_copy(xfst[:, r, :], pXT[:, r, :]),
             inc=(sXE, 1))
        b.wait("sync", sXE, t + 1)
        b.dma("sync", sXS[r],
              xfm_d.rearrange("(c p) t -> p c t", p=128)[:, :, t * 128:(t + 1) * 128],
              xfst[:, r, :].rearrange("p (c t) -> p c t", c=8))
    b.wait(g, sXS[0])
    b.wait(g, sXS[1])
    b.wait(g, sLD)
    p0c.close()

    if upto == "p0":
        _finish(nc, b, ctx)
        return

    atm_c = ExitStack()
    attn_tm = SB(atm_c, "attn_tm", [128, NTS, D], BF16)
    ac = ExitStack()
    masks = SB(ac, "masks", [128, 4, 512], BF16)
    wqkT = SB(ac, "wqkT", [128, 3, 8, 128], BF16)
    wst = SB(ac, "wst", [128, D], BF16)
    qsb = SB(ac, "qsb", [128, N], BF16)
    ksb = SB(ac, "ksb", [128, N], BF16)
    vsb = SB(ac, "vsb", [128, NTS, 128], BF16)
    xfm_sb = SB(ac, "xfm_sb", [128, 8, 512], BF16)
    s_row = SB(ac, "s_row", [128, 2, 2, N], F32)
    p2row = SB(ac, "p2row", [128, 2, N], BF16)
    e1t = SB(ac, "e1t", [128, 2, 512], F32)
    p2T = SB(ac, "p2T", [128, 2, 8, 128], BF16)
    zfr = SB(ac, "zfr", [128, 2, 8], F32)
    sfr = SB(ac, "sfr", [128, 2, 8], F32)
    bmt = SB(ac, "bmt", [128, 14, 2], F32)
    beta = SB(ac, "beta", [128, 2, 2], F32)
    z2t = SB(ac, "z2t", [128, 2, 2], F32)
    rz2t = SB(ac, "rz2t", [128, 2, 2], F32)
    pA = PS(ac, "pA", [128, 4, 512], F32)
    pB = PS(ac, "pB", [128, 2, 1024], BF16)
    pWO = PS(ac, "pWO", [128, 2, 512], F32)

    sWL = b.sem("awl"); sWT = b.sem("awt"); sWE = b.sem("awe")
    sXL2 = b.sem("axl"); sPJ = b.sem("apj"); sPJE = b.sem("apje")
    sQK = b.sem("aqk"); sMD = b.sem("amd"); sE1 = b.sem("ae1"); sST = b.sem("ast")
    sBM = b.sem("abm"); sLn = b.sem("aln"); sB2 = b.sem("ab2")
    sX2 = b.sem("ax2"); sRZ = b.sem("arz")
    sTR = b.sem("atr"); sTE = b.sem("ate"); sPV = b.sem("apv"); sNM = b.sem("anm")

    pa_free = [None] * 4
    pwo_free = [None, None]
    pb_free = [None, None]
    last_stt = [0, 0]
    tile_ctr = [0]

    sMSK = b.sem("amsk")
    for off in range(4):
        b.do("gpsimd", lambda E, off=off: E.memset(masks[:, off, :], 0.0),
             inc=(sMSK, 1))
        b.do("gpsimd", lambda E, off=off: E.affine_select(
            out=masks[:, off, :], in_=masks[:, off, :], compare_op=ALU.is_ge,
            fill=NEG, base=off * 128, pattern=[[-1, 512]], channel_multiplier=1),
            inc=(sMSK, 1))
    nMSK = sMSK.n
    b.wait("vector", sMSK, nMSK)

    for hg in range(NH // 2):
        rows = slice(hg * 128, hg * 128 + 128)
        for m, wd in enumerate((wq_d, wk_d, wv_d)):
            if hg + m > 0:
                b.wait("gpsimd", sWT, sWT.n)
            b.dma("gpsimd", sWL, wst[:, :], wd[rows, :])
            b.wait("tensor", sWL, sWL.n)
            for half in range(2):
                bank = half
                b.twait("tensor", pb_free[bank])
                for c4 in range(4):
                    c = half * 4 + c4
                    b.do("tensor", lambda E, c=c, c4=c4, bank=bank: E.transpose(
                        pB[:, bank, c4 * 128:(c4 + 1) * 128],
                        wst[:, c * 128:(c + 1) * 128], identb[:, :]),
                        inc=(sWT, 1) if c4 == 3 else None)
                b.wait("vector", sWT, sWT.n)
                b.do("vector", lambda E, m=m, half=half, bank=bank: E.tensor_copy(
                    wqkT[:, m, half * 4:half * 4 + 4, :].rearrange("p a t -> p (a t)"),
                    pB[:, bank, 0:512]), inc=(sWE, 1))
                pb_free[bank] = (sWE, sWE.n)
        b.wait("vector", sWE, sWE.n)
        b.do("vector", lambda E: E.tensor_scalar(
            out=wqkT[:, 0, :, :].rearrange("p a t -> p (a t)"),
            in0=wqkT[:, 0, :, :].rearrange("p a t -> p (a t)"),
            scalar1=0.125, scalar2=None, op0=ALU.mult), inc=(sWE, 1))
        b.wait("tensor", sWE, sWE.n)

        for tg in range(8):
            if hg + tg > 0:
                b.wait("gpsimd", sPJ)
            b.dma("gpsimd", sXL2, xfm_sb[:, :, :],
                  xfm_d.rearrange("(c p) t -> p c t", p=128)[:, :, tg * 512:(tg + 1) * 512])
            b.wait("tensor", sXL2)
            for m, dst in ((0, qsb), (1, ksb)):
                b.twait("tensor", pwo_free[m])
                for c in range(8):
                    b.do("tensor", lambda E, m=m, c=c: E.matmul(
                        pWO[:, m, :], wqkT[:, m, c, :], xfm_sb[:, c, :],
                        start=(c == 0), stop=(c == 7)),
                        inc=(sPJ, 1) if c == 7 else None)
                b.wait("vector", sPJ, sPJ.n)
                b.do("vector", lambda E, dst=dst, m=m, tg=tg: E.tensor_copy(
                    dst[:, tg * 512:(tg + 1) * 512], pWO[:, m, :]), inc=(sPJE, 1))
                pwo_free[m] = (sPJE, sPJE.n)
            for sub in range(4):
                ts_ = tg * 4 + sub
                bank = sub % 2
                b.twait("tensor", pwo_free[bank])
                for c in range(8):
                    b.do("tensor", lambda E, c=c, sub=sub, bank=bank: E.matmul(
                        pWO[:, bank, 0:128], xfm_sb[:, c, sub * 128:(sub + 1) * 128],
                        wqkT[:, 2, c, :], start=(c == 0), stop=(c == 7)),
                        inc=(sPJ, 1) if c == 7 else None)
                b.wait("vector", sPJ, sPJ.n)
                b.do("vector", lambda E, ts_=ts_, bank=bank: E.tensor_copy(
                    vsb[:, ts_, :], pWO[:, bank, 0:128]), inc=(sPJE, 1))
                pwo_free[bank] = (sPJE, sPJE.n)
        b.wait("tensor", sPJE)

        for qi in range(NTS):
            par = qi % 2
            nk = qi // 4 + 1
            off = qi % 4
            gh = 2 * hg
            x2_need = 2 * (qi - 1) if qi >= 2 else 0

            for kj in range(nk):
                diag = (kj == nk - 1)
                bank0 = 2 * (tile_ctr[0] % 2)
                tile_ctr[0] += 1
                banks = (bank0, bank0 + 1)
                for h01 in range(2):
                    b.twait("tensor", pa_free[banks[h01]])
                    b.do("tensor", lambda E, h01=h01, qi=qi, kj=kj, bank=banks[h01]:
                         E.matmul(pA[:, bank, :],
                                  qsb[h01 * 64:h01 * 64 + 64, qi * 128:(qi + 1) * 128],
                                  ksb[h01 * 64:h01 * 64 + 64, kj * 512:(kj + 1) * 512],
                                  start=True, stop=True), inc=(sQK, 1))
                for h01 in range(2):
                    bank = banks[h01]
                    if diag:
                        b.wait("vector", sQK, sQK.n)
                        if x2_need:
                            b.wait("vector", sX2, x2_need)
                        b.do("vector", lambda E, h01=h01, par=par, kj=kj, off=off,
                             bank=bank: E.scalar_tensor_tensor(
                                 out=s_row[:, par, h01, kj * 512:(kj + 1) * 512],
                                 in0=pA[:, bank, :], scalar=1.0,
                                 in1=masks[:, off, :], op0=ALU.mult, op1=ALU.add),
                             inc=(sMD, 1))
                        pa_free[bank] = (sMD, sMD.n)
                        b.wait("scalar", sMD, sMD.n)
                        if last_stt[h01]:
                            b.wait("scalar", sST, last_stt[h01])
                        b.do("scalar", lambda E, h01=h01, par=par, kj=kj: E.activation(
                            out=e1t[:, h01, :],
                            in_=s_row[:, par, h01, kj * 512:(kj + 1) * 512],
                            func=AF.Exp, accum_out=zfr[:, h01, kj:kj + 1]),
                            inc=(sE1, 1))
                    else:
                        b.wait("scalar", sQK, sQK.n)
                        if x2_need:
                            b.wait("scalar", sX2, x2_need)
                        b.do("scalar", lambda E, h01=h01, par=par, kj=kj, bank=bank:
                             E.copy(s_row[:, par, h01, kj * 512:(kj + 1) * 512],
                                    pA[:, bank, :]))
                        if last_stt[h01]:
                            b.wait("scalar", sST, last_stt[h01])
                        b.do("scalar", lambda E, h01=h01, kj=kj, bank=bank: E.activation(
                            out=e1t[:, h01, :], in_=pA[:, bank, :],
                            func=AF.Exp, accum_out=zfr[:, h01, kj:kj + 1]),
                            inc=(sE1, 1))
                        pa_free[bank] = (sE1, sE1.n)
                    b.wait("vector", sE1, sE1.n)
                    b.do("vector", lambda E, h01=h01, par=par, kj=kj:
                         E.scalar_tensor_tensor(
                             out=junk[:, :], in0=e1t[:, h01, :], scalar=1.0,
                             in1=s_row[:, par, h01, kj * 512:(kj + 1) * 512],
                             op0=ALU.mult, op1=ALU.mult,
                             accum_out=sfr[:, h01, kj:kj + 1]), inc=(sST, 1))
                    last_stt[h01] = sST.n

            t = bmt
            b.wait("vector", sST, sST.n)
            b.wait("vector", sE1, sE1.n)
            b.do("vector", lambda E, nk=nk: E.reduce_sum(t[:, 0, :], zfr[:, :, 0:nk],
                                                         axis=AX.X), inc=(sBM, 1))
            b.do("vector", lambda E, nk=nk: E.reduce_sum(t[:, 1, :], sfr[:, :, 0:nk],
                                                         axis=AX.X), inc=(sBM, 1))
            b.wait("vector", sBM, sBM.n)
            b.do("vector", lambda E: E.reciprocal(t[:, 2, :], t[:, 0, :]), inc=(sBM, 1))
            b.wait("scalar", sBM, sBM.n - 1)
            b.do("scalar", lambda E: E.activation(out=t[:, 3, :], in_=t[:, 0, :],
                                                  func=AF.Ln), inc=(sLn, 1))
            b.wait("vector", sBM, sBM.n)
            b.wait("vector", sLn, sLn.n)
            if x2_need:
                b.wait("vector", sX2, x2_need)
            a0, a1, a2, a3, a4 = POLY
            def step(emit):
                b.do("vector", emit, inc=(sBM, 1))
                b.wait("vector", sBM, sBM.n)
            step(lambda E: E.tensor_mul(t[:, 4, :], t[:, 1, :], t[:, 2, :]))
            step(lambda E: E.tensor_sub(t[:, 5, :], t[:, 3, :], t[:, 4, :]))
            step(lambda E: E.tensor_mul(t[:, 6, :], t[:, 5, :], t[:, 5, :]))
            def s4(E):
                E.tensor_scalar(out=t[:, 7, :], in0=t[:, 6, :], scalar1=a0,
                                scalar2=a2, op0=ALU.mult, op1=ALU.add)
                E.tensor_scalar(out=t[:, 8, :], in0=t[:, 5, :], scalar1=a1,
                                scalar2=None, op0=ALU.mult)
                E.tensor_scalar(out=t[:, 9, :], in0=t[:, 5, :], scalar1=a3,
                                scalar2=a4, op0=ALU.mult, op1=ALU.add)
                return E.tensor_scalar(out=t[:, 10, :], in0=t[:, 5, :], scalar1=0.5,
                                       scalar2=None, op0=ALU.is_gt)
            step(s4)
            step(lambda E: E.tensor_add(t[:, 11, :], t[:, 7, :], t[:, 8, :]))
            step(lambda E: E.tensor_mul(t[:, 12, :], t[:, 11, :], t[:, 6, :]))
            step(lambda E: E.tensor_add(t[:, 13, :], t[:, 12, :], t[:, 9, :]))
            step(lambda E: E.tensor_scalar(out=t[:, 7, :], in0=t[:, 13, :], scalar1=1.0,
                                           scalar2=-1.0, op0=ALU.max, op1=ALU.add))
            step(lambda E: E.tensor_mul(t[:, 8, :], t[:, 7, :], t[:, 10, :]))
            b.do("vector", lambda E, par=par: E.tensor_scalar(
                out=beta[:, par, :], in0=t[:, 8, :], scalar1=1.0, scalar2=None,
                op0=ALU.add), inc=(sB2, 1))

            rowlen = (qi + 1) * 128
            b.wait("scalar", sB2, sB2.n)
            if qi >= 2:
                b.wait("scalar", sRZ, qi - 1)
            for h01 in range(2):
                if qi >= 1:
                    b.wait("scalar", sTR, sTR.n)
                b.do("scalar", lambda E, h01=h01, par=par, rowlen=rowlen: E.activation(
                    out=p2row[:, h01, 0:rowlen], in_=s_row[:, par, h01, 0:rowlen],
                    func=AF.Exp, scale=beta[:, par, h01:h01 + 1],
                    accum_out=z2t[:, par, h01:h01 + 1]), inc=(sX2, 1))
            b.wait("vector", sX2, sX2.n)
            b.do("vector", lambda E, par=par: E.reciprocal(rz2t[:, par, :],
                                                           z2t[:, par, :]), inc=(sRZ, 1))

            nkc = qi + 1
            nbatch = (nkc + 7) // 8
            if qi >= 2:
                b.wait("tensor", sNM, 2 * (qi - 1))
            for h01 in range(2):
                bank = h01
                b.wait("tensor", sX2, sX2.n - 1 + h01)
                for bt in range(nbatch):
                    k0 = bt * 8
                    kn = min(8, nkc - k0)
                    b.twait("tensor", pb_free[bank])
                    for s in range(kn):
                        b.do("tensor", lambda E, h01=h01, s=s, k0=k0, bank=bank:
                             E.transpose(pB[:, bank, s * 128:(s + 1) * 128],
                                         p2row[:, h01, (k0 + s) * 128:(k0 + s + 1) * 128],
                                         identb[:, :]),
                             inc=(sTR, 1) if s == kn - 1 else None)
                    b.wait("vector", sTR, sTR.n)
                    if bt > 0:
                        b.wait("vector", sPV, sPV.n)
                    b.do("vector", lambda E, h01=h01, kn=kn, bank=bank: E.tensor_copy(
                        p2T[:, h01, 0:kn, :].rearrange("p a t -> p (a t)"),
                        pB[:, bank, 0:kn * 128]), inc=(sTE, 1))
                    pb_free[bank] = (sTE, sTE.n)
                    b.wait("tensor", sTE, sTE.n)
                    for s in range(kn):
                        kc = k0 + s
                        b.do("tensor", lambda E, h01=h01, s=s, kc=kc, par=par, nkc=nkc:
                             E.matmul(pWO[:, par, h01 * 64:h01 * 64 + 64],
                                      p2T[:, h01, s, :],
                                      vsb[:, kc, h01 * 64:h01 * 64 + 64],
                                      start=(kc == 0), stop=(kc == nkc - 1)),
                             inc=(sPV, 1) if s == kn - 1 else None)
            b.wait("vector", sPV, sPV.n)
            b.wait("vector", sRZ, sRZ.n)
            for h01 in range(2):
                b.do("vector", lambda E, h01=h01, par=par, qi=qi, gh=gh: E.tensor_scalar(
                    out=attn_tm[:, qi, (gh + h01) * 64:(gh + h01) * 64 + 64],
                    in0=pWO[:, par, h01 * 64:h01 * 64 + 64],
                    scalar1=rz2t[:, par, h01:h01 + 1], scalar2=None, op0=ALU.mult),
                    inc=(sNM, 1))
            pwo_free[par] = (sNM, sNM.n)

    nNM = sNM.n

    if debug_scratch:
        sAD = b.sem("adbg")
        b.wait("sync", sNM, nNM)
        for t_ in range(NTS):
            b.dma("sync", sAD, atm_d[t_ * 128:(t_ + 1) * 128, :], attn_tm[:, t_, :])
        for nm, tile in (("qsb", qsb[:, :]), ("ksb", ksb[:, :]),
                         ("p2row", p2row[:, :, :].rearrange("p a t -> p (a t)"))):
            dshape = [tile.shape[0], int(np.prod(tile.shape[1:]))]
            dd = nc.dram_tensor("dbg_" + nm, dshape, BF16, kind="ExternalOutput").ap()
            b.dma("sync", sAD, dd, tile)
        for nm, tile in (("vsb", vsb),):
            dd = nc.dram_tensor("dbg_" + nm, [128, NTS * 128], BF16,
                                kind="ExternalOutput").ap()
            b.dma("sync", sAD, dd, tile[:, :, :].rearrange("p a t -> p (a t)"))
        for nm, tile in (("s_row", s_row), ("zfr", zfr), ("sfr", sfr),
                         ("bmt", bmt), ("beta", beta), ("z2t", z2t)):
            dshape = [tile.shape[0], int(np.prod(tile.shape[1:]))]
            dd = nc.dram_tensor("dbg_" + nm, dshape, F32, kind="ExternalOutput").ap()
            flat = tile
            if len(tile.shape) == 3:
                flat = tile[:, :, :].rearrange("p a t -> p (a t)")
            elif len(tile.shape) == 4:
                flat = tile[:, :, :, :].rearrange("p a b t -> p (a b t)")
            b.dma("sync", sAD, dd, flat)
        b.wait("sync", sAD)

    ac.close()
    if upto == "attn":
        atm_c.close()
        _finish(nc, b, ctx)
        return

    p3c = ExitStack()
    ones1b = SB(p3c, "ones1b", [1, 128], BF16)
    bo_row = SB(p3c, "bo_row", [1, D], BF16)
    br_col = SB(p3c, "br_col", [NE, 1], F32)
    wr_sb = SB(p3c, "wr_sb", [128, 8, NE], F32)
    woT = SB(p3c, "woT", [128, 8, D], BF16)
    wo_nat = SB(p3c, "wo_nat", [128, 8, D], BF16)
    afm = SB(p3c, "afm", [128, 2, 1024], BF16)
    xres = SB(p3c, "xres", [128, 2, D], F32)
    s1f = SB(p3c, "s1f", [128, 2, D], F32)
    r1tmb = SB(p3c, "r1tmb", [128, 2, D], BF16)
    r1f = SB(p3c, "r1f", [128, 2, D], F32)
    r1fmf = SB(p3c, "r1fmf", [128, 2, 8, 128], F32)
    r1fmb = SB(p3c, "r1fmb", [128, 2, 8, 128], BF16)
    router_sb = SB(p3c, "router_sb", [NE, N], F32)
    pR3 = PS(p3c, "pR3", [128, 2, 512], F32)
    p3ps = ExitStack()
    pT3 = PS(p3ps, "pT3", [128, 1024], BF16)
    pWo3 = PS(p3ps, "pWo3", [128, 2, 512], F32)
    pRT3 = PS(p3ps, "pRT3", [128, 2, 512], F32)

    sWoL = b.sem("woL"); sT3 = b.sem("t3"); sAE3 = b.sem("ae3"); sWO3 = b.sem("wo3")
    sXR3 = [b.sem("xr3a"), b.sem("xr3b")]
    sS13 = b.sem("s13"); sSt3 = b.sem("st3"); sVr3 = b.sem("vr3")
    sRS3 = b.sem("rs3"); sNM3 = b.sem("nm3")
    sSp3 = [b.sem("sp3a"), b.sem("sp3b")]
    sRT3 = b.sem("rt3"); sRE3 = b.sem("re3")
    sSp4 = [b.sem("sp4a"), b.sem("sp4b")]
    sRM3 = b.sem("rm3"); sRV3 = b.sem("rv3")

    b.wait("gpsimd", sNM, nNM)
    sONE = b.sem("p3one")
    b.do("gpsimd", lambda E: E.memset(ones1b[:, :], 1.0), inc=(sONE, 1))
    b.wait("tensor", sONE, 1)
    b.dma("gpsimd", sWoL, bo_row[:, :], bo_d)
    b.dma("gpsimd", sWoL, br_col[:, :], br_d)
    b.dma("gpsimd", sWoL, wr_sb[:, :, :], wr_d.rearrange("(c p) e -> p c e", p=128))
    b.dma("gpsimd", sWoL, wo_nat[:, :, :], wo_d.rearrange("(c p) d -> p c d", p=128))
    b.wait("tensor", sWoL)
    b.wait("vector", sWoL)
    b.wait("tensor", sNM, nNM)
    for cd in range(8):
        if cd >= 1:
            b.wait("tensor", sAE3, cd)
        for c in range(8):
            b.do("tensor", lambda E, c=c, cd=cd: E.transpose(
                pT3[:, c * 128:(c + 1) * 128], wo_nat[:, c, cd * 128:(cd + 1) * 128],
                identb[:, :]), inc=(sT3, 1) if c == 7 else None)
        b.wait("vector", sT3, sT3.n)
        b.do("vector", lambda E, cd=cd: E.tensor_copy(woT[:, cd, :], pT3[:, :]),
             inc=(sAE3, 1))

    if upto == "p3w":
        b.wait("vector", sAE3, sAE3.n)
        p3ps.close()
        p3c.close()
        _finish(nc, b, ctx)
        return

    def emit_ln_tail(semv, sema):
        def stepv(emit):
            b.do("vector", emit, inc=(semv, 1))
            b.wait("vector", semv, semv.n)
        def v1(E):
            E.tensor_scalar(out=stt_[:, 3:4], in0=stt_[:, 0:1], scalar1=1.0 / D,
                            scalar2=None, op0=ALU.mult)
            return E.tensor_add(stt_[:, 4:5], stt_[:, 1:2], stt_[:, 2:3])
        stepv(v1)
        def v2(E):
            E.tensor_scalar(out=stt_[:, 5:6], in0=stt_[:, 4:5], scalar1=1.0 / D,
                            scalar2=None, op0=ALU.mult)
            return E.tensor_mul(stt_[:, 6:7], stt_[:, 3:4], stt_[:, 3:4])
        stepv(v2)
        b.do("vector", lambda E: E.tensor_sub(stt_[:, 4:5], stt_[:, 5:6],
                                              stt_[:, 6:7]), inc=(semv, 1))
        b.wait("scalar", semv, semv.n)
        b.do("scalar", lambda E: E.activation(out=stt_[:, 6:7], in_=stt_[:, 4:5],
                                              func=AF.Ln, bias=c_eps[:, 0:1]),
             inc=(sema, 1))
        b.wait("scalar", sema, sema.n)
        b.do("scalar", lambda E: E.activation(out=stt_[:, 7:8], in_=stt_[:, 6:7],
                                              func=AF.Exp, scale=-0.5), inc=(sema, 1))
        b.wait("vector", sema, sema.n)

    n_ts_p3 = int(os.environ.get("K_P3_N", NTS))
    for ts_ in range(n_ts_p3):
        r = ts_ % 2
        if ts_ >= 2:
            b.wait("gpsimd", sS13, ts_ - 1)
        b.dma("gpsimd", sXR3[r], xres[:, r, :], x_d[ts_ * 128:(ts_ + 1) * 128, :])
        if os.environ.get("K_P3_CUT") == "dma":
            b.do("vector", lambda E: E.tensor_copy(junk[:, 0:8], stt_[:, :]),
                 inc=(sRV3, 1))
            b.do("vector", lambda E, r=r: E.tensor_copy(junk[:, 8:16], xres[:, r, 0:8]),
                 inc=(sS13, 1))
            b.wait("vector", sXR3[r], sXR3[r].n)
            continue
        b.wait("tensor", sAE3, sAE3.n)
        for c in range(8):
            b.do("tensor", lambda E, c=c, ts_=ts_: E.transpose(
                pT3[:, c * 128:(c + 1) * 128], attn_tm[:, ts_, c * 128:(c + 1) * 128],
                identb[:, :]), inc=(sT3, 1) if c == 7 else None)
        b.wait("vector", sT3, sT3.n)
        if ts_ >= 2:
            b.wait("vector", sWO3, 2 * (ts_ - 1))
        b.do("vector", lambda E, r=r: E.tensor_copy(
            afm[:, r, :], pT3[:, :]), inc=(sAE3, 1))
        if os.environ.get("K_P3_CUT") == "tr":
            b.do("vector", lambda E: E.tensor_copy(junk[:, 0:8], stt_[:, :]),
                 inc=(sRV3, 1))
            b.do("vector", lambda E, r=r: E.tensor_copy(junk[:, 0:8], afm[:, r, 0:8]),
                 inc=(sS13, 1))
            continue
        b.wait("tensor", sAE3, sAE3.n)
        if ts_ >= 1:
            b.wait("tensor", sS13, ts_)
        for half in range(2):
            for c in range(8):
                b.do("tensor", lambda E, c=c, r=r, half=half: E.matmul(
                    pWo3[:, half, :], afm[:, r, c * 128:(c + 1) * 128],
                    woT[:, c, half * 512:(half + 1) * 512],
                    start=(c == 0), stop=False))
            b.do("tensor", lambda E, half=half: E.matmul(
                pWo3[:, half, :], ones1b[:, :], bo_row[:, half * 512:(half + 1) * 512],
                start=False, stop=True), inc=(sWO3, 1))
        b.wait("vector", sWO3, sWO3.n)
        b.wait("vector", sXR3[r], sXR3[r].n)
        if ts_ >= 2:
            b.wait("vector", sNM3, 2 * (ts_ - 1))
        if os.environ.get("K_P3_CUT") == "wo":
            b.do("vector", lambda E, r=r: E.tensor_add(
                s1f[:, r, 0:512], xres[:, r, 0:512], pWo3[:, 0, :]), inc=(sS13, 1))
            b.do("vector", lambda E: E.tensor_copy(junk[:, 0:8], stt_[:, :]),
                 inc=(sRV3, 1))
            continue
        if os.environ.get("K_P3_CUT") == "nob2":
            b.do("vector", lambda E, r=r: E.tensor_add(
                s1f[:, r, :], xres[:, r, :],
                pWo3[:, :, :].rearrange("p a t -> p (a t)")), inc=(sS13, 1))
            b.do("vector", lambda E: E.tensor_copy(junk[:, 0:8], stt_[:, :]),
                 inc=(sRV3, 1))
            continue
        b.do("vector", lambda E, r=r: E.tensor_add(
            s1f[:, r, :], xres[:, r, :], pWo3[:, :, :].rearrange("p a t -> p (a t)")),
            inc=(sS13, 1))
        b.do("vector", lambda E, r=r: E.reduce_sum(stt_[:, 0:1], s1f[:, r, :], axis=AX.X))
        b.do("vector", lambda E, r=r: E.scalar_tensor_tensor(
            out=junk[:, :], in0=s1f[:, r, 0:512], scalar=1.0, in1=s1f[:, r, 0:512],
            op0=ALU.mult, op1=ALU.mult, accum_out=stt_[:, 1:2]))
        b.do("vector", lambda E, r=r: E.scalar_tensor_tensor(
            out=junk[:, :], in0=s1f[:, r, 512:1024], scalar=1.0, in1=s1f[:, r, 512:1024],
            op0=ALU.mult, op1=ALU.mult, accum_out=stt_[:, 2:3]), inc=(sSt3, 1))
        b.wait("vector", sSt3, sSt3.n)
        if os.environ.get("K_P3_CUT") == "stats":
            b.do("vector", lambda E: E.tensor_copy(junk[:, 0:8], stt_[:, :]),
                 inc=(sRV3, 1))
            continue
        emit_ln_tail(sVr3, sRS3)
        if ts_ >= 2:
            b.wait("vector", sSp3[r], sSp3[r].n)
            b.wait("vector", sRT3, 2 * (ts_ - 1))
            b.wait("vector", sRM3, ts_ - 1)
            b.wait("vector", sSp4[r], sSp4[r].n)
        b.do("vector", lambda E, r=r: E.tensor_scalar(
            out=r1tmb[:, r, :], in0=s1f[:, r, :], scalar1=stt_[:, 3:4],
            scalar2=stt_[:, 7:8], op0=ALU.subtract, op1=ALU.mult), inc=(sNM3, 1))
        b.do("vector", lambda E, r=r: E.tensor_scalar(
            out=r1f[:, r, :], in0=s1f[:, r, :], scalar1=stt_[:, 3:4],
            scalar2=stt_[:, 7:8], op0=ALU.subtract, op1=ALU.mult), inc=(sNM3, 1))
        if os.environ.get("K_P3_CUT") == "norm":
            b.do("vector", lambda E: E.tensor_copy(junk[:, 0:8], stt_[:, :]),
                 inc=(sRV3, 1))
            continue
        b.wait("sync", sNM3, sNM3.n - 1)
        b.dma("sync", sSp3[r], r1tm_d[ts_ * 128:(ts_ + 1) * 128, :], r1tmb[:, r, :])
        b.wait("tensor", sNM3, sNM3.n)
        if ts_ >= 1:
            b.wait("tensor", sRE3, 2 * ts_)
        for c in range(8):
            b.do("tensor", lambda E, c=c, r=r: E.transpose(
                pR3[:, c // 4, (c % 4) * 128:(c % 4 + 1) * 128],
                r1f[:, r, c * 128:(c + 1) * 128], identf[:, :]),
                inc=(sRT3, 1) if c in (3, 7) else None)
        b.wait("vector", sRT3, sRT3.n)
        b.do("vector", lambda E, r=r: E.tensor_copy(
            r1fmf[:, r, :, :].rearrange("p a t -> p (a t)"),
            pR3[:, :, :].rearrange("p a t -> p (a t)")), inc=(sRE3, 1))
        b.do("vector", lambda E, r=r: E.tensor_copy(
            r1fmb[:, r, :, :].rearrange("p a t -> p (a t)"),
            pR3[:, :, :].rearrange("p a t -> p (a t)")), inc=(sRE3, 1))
        b.wait("sync", sRE3, sRE3.n)
        b.dma("sync", sSp4[r],
              r1fm_d.rearrange("(c p) t -> p c t", p=128)[:, :, ts_ * 128:(ts_ + 1) * 128],
              r1fmb[:, r, :, :])
        if os.environ.get("K_P3_CUT") == "spill":
            b.do("vector", lambda E: E.tensor_copy(junk[:, 0:8], stt_[:, :]),
                 inc=(sRV3, 1))
            continue
        b.wait("tensor", sRE3, sRE3.n - 1)
        if ts_ >= 2:
            b.wait("tensor", sRV3, ts_ - 1)
        for c in range(8):
            b.do("tensor", lambda E, c=c, r=r: E.matmul(
                pRT3[0:NE, r, 0:128], wr_sb[:, c, :], r1fmf[:, r, c, :],
                start=(c == 0), stop=(c == 7)), inc=(sRM3, 1) if c == 7 else None)
        b.wait("vector", sRM3, sRM3.n)
        b.do("vector", lambda E, r=r, ts_=ts_: E.tensor_scalar(
            out=router_sb[:, ts_ * 128:(ts_ + 1) * 128], in0=pRT3[0:NE, r, 0:128],
            scalar1=br_col[:, 0:1], scalar2=None, op0=ALU.add), inc=(sRV3, 1))
    b.wait("sync", sSp3[0])
    b.wait("sync", sSp3[1])
    b.wait("sync", sSp4[0])
    b.wait("sync", sSp4[1])
    if upto == "p3l":
        b.wait("vector", sRV3, sRV3.n)
        p3ps.close()
        p3c.close()
        _finish(nc, b, ctx)
        return
    nRV3 = sRV3.n
    nRE3 = sRE3.n
    p3ps.close()

    rpc = ExitStack()
    lg_tm = SB(rpc, "lg_tm", [128, NTS, NE], F32)
    e8 = SB(rpc, "e8", [128, NTS, NE], F32)
    zr = SB(rpc, "zr", [128, NTS], F32)
    lmax = SB(rpc, "lmax", [128, NTS], F32)
    rzr = SB(rpc, "rzr", [128, NTS], F32)
    gate = SB(rpc, "gate", [128, NTS], F32)
    mg_tm = SB(rpc, "mg_tm", [128, NTS, NE], BF16)
    mg_sb = SB(rpc, "mg_sb", [NE, N], BF16)
    pM3 = PS(rpc, "pM3", [NE, 4, 1024], BF16)

    sRP = b.sem("rp"); sRPE = b.sem("rpe"); sRPA = b.sem("rpa")
    sMG = b.sem("mg"); sMGE = b.sem("mge"); sMGS = b.sem("mgs")

    b.wait("tensor", sRV3, nRV3)
    b.wait("tensor", sRE3, nRE3)
    for ch in range(NTS):
        b.do("tensor", lambda E, ch=ch: E.transpose(
            pR3[:, ch // 16, (ch % 16) * 8:(ch % 16) * 8 + 8],
            router_sb[:, ch * 128:(ch + 1) * 128], identf[0:NE, 0:NE]),
            inc=(sRP, 1) if ch in (15, 31) else None)
        if ch == 15:
            b.wait("vector", sRP, 1)
            b.do("vector", lambda E: E.tensor_copy(
                lg_tm[:, 0:16, :].rearrange("p a t -> p (a t)"), pR3[:, 0, 0:128]),
                inc=(sRPE, 1))
        if ch == 31:
            b.wait("vector", sRP, 2)
            b.do("vector", lambda E: E.tensor_copy(
                lg_tm[:, 16:32, :].rearrange("p a t -> p (a t)"), pR3[:, 1, 0:128]),
                inc=(sRPE, 1))
    b.wait("scalar", sRPE, 2)
    b.do("scalar", lambda E: E.activation(
        out=e8[:, :, :].rearrange("p a t -> p (a t)"),
        in_=lg_tm[:, :, :].rearrange("p a t -> p (a t)"), func=AF.Exp), inc=(sRPA, 1))
    b.wait("vector", sRPA, 1)
    b.do("vector", lambda E: E.reduce_sum(zr[:, :], e8[:, :, :], axis=AX.X),
         inc=(sRPA, 1))
    b.do("vector", lambda E: E.reduce_max(lmax[:, :], lg_tm[:, :, :], axis=AX.X),
         inc=(sRPA, 1))
    b.do("vector", lambda E: E.reduce_max(gate[:, :], e8[:, :, :], axis=AX.X),
         inc=(sRPA, 1))
    b.wait("vector", sRPA, 4)
    b.do("vector", lambda E: E.reciprocal(rzr[:, :], zr[:, :]), inc=(sRPA, 1))
    b.wait("vector", sRPA, 5)
    b.do("vector", lambda E: E.tensor_mul(gate[:, :], gate[:, :], rzr[:, :]),
         inc=(sRPA, 1))
    b.wait("vector", sRPA, 6)
    for ch in range(NTS):
        b.do("vector", lambda E, ch=ch: E.tensor_scalar(
            out=mg_tm[:, ch, :], in0=lg_tm[:, ch, :], scalar1=lmax[:, ch:ch + 1],
            scalar2=gate[:, ch:ch + 1], op0=ALU.is_ge, op1=ALU.mult),
            inc=(sRPA, 1) if ch == NTS - 1 else None)
    b.wait("tensor", sRPA, 7)
    for ch in range(NTS):
        b.do("tensor", lambda E, ch=ch: E.transpose(
            pM3[:, ch // 8, (ch % 8) * 128:(ch % 8) * 128 + 128],
            mg_tm[:, ch, :], identb[:, :]),
            inc=(sMG, 1) if ch % 8 == 7 else None)
    b.wait("vector", sMG, 4)
    for bk in range(4):
        b.do("vector", lambda E, bk=bk: E.tensor_copy(
            mg_sb[:, bk * 1024:(bk + 1) * 1024], pM3[:, bk, :]), inc=(sMGE, 1))
    b.wait("sync", sMGE, 4)
    b.dma("sync", sMGS, mg_d[:, :], mg_sb[:, :])
    b.wait("sync", sMGS, 16)
    b.wait("gpsimd", sMGS, 16)
    rpc.close()
    p3c.close()
    atm_c.close()

    if upto == "p3":
        _finish(nc, b, ctx)
        return

    p4c = ExitStack()
    b1cols = SB(p4c, "b1cols", [128, NE, 32], F32)
    b2_sb = SB(p4c, "b2_sb", [NE, D], BF16)
    sel_sb = SB(p4c, "sel_sb", [NE, NE * 128], BF16)
    w1_sb = SB(p4c, "w1_sb", [128, 8, DFF], BF16)
    w2_sb = SB(p4c, "w2_sb", [128, 32, D], BF16)
    r1q = SB(p4c, "r1q", [128, 8, 1024], BF16)
    acc = SB(p4c, "acc", [128, 8, 1024], BF16)
    h1pre = SB(p4c, "h1pre", [128, 4, 256], BF16)
    h1m = SB(p4c, "h1m", [128, 4, 256], BF16)
    mgb = SB(p4c, "mgb", [128, 4, 256], BF16)
    mg_q = SB(p4c, "mg_q", [NE, 1024], BF16)
    r1tmq = SB(p4c, "r1tmq", [128, 2, D], BF16)
    s2f = SB(p4c, "s2f", [128, D], F32)
    outst = SB(p4c, "outst", [128, 2, D], F32)
    pYe = PS(p4c, "pYe", [128, 4, 512], F32)
    pH1 = PS(p4c, "pH1", [128, 2, 512], F32)
    pL2 = PS(p4c, "pL2", [128, 1024], BF16)
    pMB = PS(p4c, "pMB", [128, 512], F32)

    sR1L = b.sem("fr1"); sMQL = b.sem("fmq")
    sW1L = b.sem("fw1"); sW2L = b.sem("fw2")
    sMB = b.sem("fmb"); sMBE = b.sem("fmbe")
    sA = b.sem("fa"); sRel = b.sem("frel"); sMsk = b.sem("fmsk")
    sBmm = b.sem("fbmm"); sYe = b.sem("fye")
    sL2T = b.sem("fl2t"); sS2 = b.sem("fs2")
    sR1T = [b.sem("fr1ta"), b.sem("fr1tb")]
    sSt4 = b.sem("fst4"); sVr4 = b.sem("fvr4"); sRS4 = b.sem("frs4")
    sNM4 = b.sem("fnm4")
    sOut = [b.sem("fouta"), b.sem("foutb")]
    sDbgS2 = b.sem("fdbgs2")

    def load_w1(e):
        for c in range(8):
            b.dma("gpsimd", sW1L, w1_sb[:, c, :],
                  w1_d[e, :, :].rearrange("(c p) f -> p c f", p=128)[:, c, :])

    def load_w2(e):
        for cc in range(4):
            b.dma("gpsimd", sW2L, w2_sb[:, cc * 8:(cc + 1) * 8, :],
                  w2_d[e, :, :].rearrange("(c p) h -> p c h", p=128)[:, cc * 8:(cc + 1) * 8, :])

    p4s = ExitStack()
    b1st = SB(p4s, "b1st", [32, NE, 128], F32)
    selst = SB(p4s, "selst", [NE, NE * 128], F32)
    selst2 = SB(p4s, "selst2", [NE, NE * 128], F32)
    sC4 = b.sem("fc4")
    sC4b = b.sem("fc4b")
    b.do("gpsimd", lambda E: E.iota(selst[:, :], [[0, NE * 128]], channel_multiplier=1,
                                    allow_small_or_imprecise_dtypes=True), inc=(sC4, 1))
    b.do("gpsimd", lambda E: E.iota(selst2[:, :], [[1, NE], [0, 128]],
                                    channel_multiplier=0,
                                    allow_small_or_imprecise_dtypes=True), inc=(sC4, 1))
    b.dma("gpsimd", sC4, b2_sb[:, :], b2_d)
    for e in range(NE):
        b.dma("gpsimd", sC4, b1st[:, e, :],
              b1_d[e:e + 1, :].rearrange("a (c p) -> (a c) p", c=32))
    nC4 = sC4.n
    b.wait("vector", sC4, nC4)
    b.do("vector", lambda E: E.tensor_tensor(out=sel_sb[:, :], in0=selst[:, :],
                                             in1=selst2[:, :], op=ALU.is_equal),
         inc=(sC4b, 1))
    b.wait("tensor", sC4, nC4)
    for e in range(NE):
        if e >= 2:
            b.wait("tensor", sC4b, e)
        b.do("tensor", lambda E, e=e: E.transpose(
            pMB[:, (e % 2) * 256:(e % 2) * 256 + 32], b1st[:, e, :],
            identf[0:32, 0:32]), inc=(sC4, 1))
        b.wait("vector", sC4, sC4.n)
        b.do("vector", lambda E, e=e: E.tensor_copy(
            b1cols[:, e, :], pMB[:, (e % 2) * 256:(e % 2) * 256 + 32]), inc=(sC4b, 1))
    b.wait("tensor", sC4b, sC4b.n)
    p4s.close()

    load_w1(0)
    load_w2(0)
    n_w1 = 1
    n_w2 = 1

    NQ = 4
    QTOK = N // NQ
    NG = QTOK // 256

    h1ps_free = [None] * 4
    h1pre_free = [None] * 4
    h1m_free = [None] * 4
    act_bank_free = [None, None]
    ye_free = None
    mgb_free = [None] * 4

    for q in range(NQ):
        if q > 0:
            b.wait("gpsimd", sA)
        b.dma("gpsimd", sR1L, r1q[:, :, :],
              r1fm_d.rearrange("(c p) t -> p c t", p=128)[:, :, q * QTOK:(q + 1) * QTOK])
        if q > 0:
            b.wait("gpsimd", sMB)
        b.dma("gpsimd", sMQL, mg_q[:, :], mg_d[:, q * QTOK:(q + 1) * QTOK])

        for e in range(NE):
            b.wait("tensor", sW1L, 16 * 8 * n_w1)
            b.wait("tensor", sW2L, 16 * 4 * n_w2)
            b.wait("tensor", sR1L, 16 * (q + 1))
            b.wait("tensor", sMQL, 16 * (q + 1))
            for gi in range(NG):
                b.do("tensor", lambda E, e=e, gi=gi: E.matmul(
                    pMB[:, (gi % 2) * 256:(gi % 2) * 256 + 256],
                    sel_sb[:, e * 128:(e + 1) * 128], mg_q[:, gi * 256:(gi + 1) * 256],
                    start=True, stop=True), inc=(sMB, 1))
                b.wait("vector", sMB, sMB.n)
                b.twait("vector", mgb_free[gi])
                b.do("vector", lambda E, gi=gi: E.tensor_copy(
                    mgb[:, gi, :], pMB[:, (gi % 2) * 256:(gi % 2) * 256 + 256]),
                    inc=(sMBE, 1))
                b.wait("tensor", sMBE, sMBE.n)

            for gi in range(NG):
                for f in range(33):
                    if f < 32:
                        slot = f % 4
                        bank = slot % 2
                        halfsel = slot // 2
                        b.twait("tensor", h1ps_free[slot])
                        b.twait("tensor", act_bank_free[bank])
                        for c in range(8):
                            b.do("tensor", lambda E, c=c, f=f, gi=gi, bank=bank,
                                 halfsel=halfsel: E.matmul(
                                     pH1[:, bank, halfsel * 256:halfsel * 256 + 256],
                                     w1_sb[:, c, f * 128:(f + 1) * 128],
                                     r1q[:, c, gi * 256:(gi + 1) * 256],
                                     start=(c == 0), stop=(c == 7)),
                                inc=(sA, 1) if c == 7 else None)
                        b.wait("scalar", sA, sA.n)
                        b.twait("scalar", h1pre_free[slot])
                        b.do("scalar", lambda E, f=f, e=e, bank=bank, halfsel=halfsel,
                             slot=slot: E.activation(
                                 out=h1pre[:, slot, :],
                                 in_=pH1[:, bank, halfsel * 256:halfsel * 256 + 256],
                                 func=AF.Relu, bias=b1cols[:, e, f:f + 1]),
                            inc=(sRel, 1))
                        h1ps_free[slot] = (sRel, sRel.n)
                        act_bank_free[bank] = (sRel, sRel.n)
                        b.wait("vector", sRel, sRel.n)
                        b.twait("vector", h1m_free[slot])
                        b.do("vector", lambda E, slot=slot, gi=gi: E.tensor_mul(
                            h1m[:, slot, :], h1pre[:, slot, :], mgb[:, gi, :]),
                            inc=(sMsk, 1))
                        h1pre_free[slot] = (sMsk, sMsk.n)
                    fb = f - 1
                    if fb >= 0:
                        slotb = fb % 4
                        b.wait("tensor", sMsk, sMsk.n if f == 32 else sMsk.n - 1)
                        if fb == 0:
                            b.twait("tensor", ye_free)
                        last_real = (fb == 31 and e != 0)
                        for h in range(8):
                            b.do("tensor", lambda E, h=h, fb=fb, slotb=slotb,
                                 last_real=last_real: E.matmul(
                                     pYe[:, h // 2, (h % 2) * 256:(h % 2) * 256 + 256],
                                     w2_sb[:, fb, h * 128:(h + 1) * 128], h1m[:, slotb, :],
                                     start=(fb == 0 and h % 2 == 0), stop=last_real),
                                inc=(sBmm, 1) if h == 7 else None)
                        if fb == 31:
                            h1m_free[slotb] = (sBmm, sBmm.n)
                        else:
                            h1m_free[slotb] = (sBmm, sBmm.n)
                if e == 0:
                    for h in range(8):
                        b.do("tensor", lambda E, h=h, gi=gi: E.matmul(
                            pYe[:, h // 2, (h % 2) * 256:(h % 2) * 256 + 256],
                            b2_sb[:, h * 128:(h + 1) * 128],
                            mg_q[:, gi * 256:(gi + 1) * 256],
                            start=False, stop=True),
                            inc=(sBmm, 1) if h == 7 else None)
                b.wait("vector", sBmm, sBmm.n)
                for h in range(8):
                    if e == 0:
                        b.do("vector", lambda E, h=h, gi=gi: E.tensor_copy(
                            acc[:, h, gi * 256:(gi + 1) * 256],
                            pYe[:, h // 2, (h % 2) * 256:(h % 2) * 256 + 256]),
                            inc=(sYe, 1))
                    else:
                        b.do("vector", lambda E, h=h, gi=gi: E.tensor_add(
                            acc[:, h, gi * 256:(gi + 1) * 256],
                            acc[:, h, gi * 256:(gi + 1) * 256],
                            pYe[:, h // 2, (h % 2) * 256:(h % 2) * 256 + 256]),
                            inc=(sYe, 1))
                ye_free = (sYe, sYe.n)
            if not (q == NQ - 1 and e == NE - 1):
                b.wait("gpsimd", sA)
                load_w1((e + 1) % NE)
                n_w1 += 1
                b.wait("gpsimd", sBmm)
                load_w2((e + 1) % NE)
                n_w2 += 1

        if debug_scratch:
            sACD = b.sem(f"accd{q}")
            b.wait("sync", sYe, sYe.n)
            b.dma("sync", sACD,
                  acc_d.rearrange("(c p) t -> p c t", p=128)[:, :, q * QTOK:(q + 1) * QTOK],
                  acc[:, :, :])
            b.wait("sync", sACD, 16)

        for t8 in range(8):
            ts_ = q * 8 + t8
            r = t8 % 2
            if sS2.n >= 1:
                b.wait("gpsimd", sS2, sS2.n - 1)
            b.dma("gpsimd", sR1T[r], r1tmq[:, r, :], r1tm_d[ts_ * 128:(ts_ + 1) * 128, :])
            b.wait("tensor", sYe, sYe.n)
            if sS2.n >= 1:
                b.wait("tensor", sS2, sS2.n)
            for c in range(8):
                b.do("tensor", lambda E, c=c, t8=t8: E.transpose(
                    pL2[:, c * 128:(c + 1) * 128], acc[:, c, t8 * 128:(t8 + 1) * 128],
                    identb[:, :]), inc=(sL2T, 1) if c == 7 else None)
            b.wait("vector", sL2T, sL2T.n)
            b.wait("vector", sR1T[r], sR1T[r].n)
            b.do("vector", lambda E, r=r: E.tensor_add(
                s2f[:, :], r1tmq[:, r, :], pL2[:, :]), inc=(sS2, 1))
            if debug_scratch:
                b.wait("sync", sS2, sS2.n)
                b.dma("sync", sDbgS2, s2_d[ts_ * 128:(ts_ + 1) * 128, :], s2f[:, :])
                b.wait("vector", sDbgS2, sDbgS2.n)
            b.do("vector", lambda E: E.reduce_sum(stt_[:, 0:1], s2f[:, :], axis=AX.X))
            b.do("vector", lambda E: E.scalar_tensor_tensor(
                out=junk[:, :], in0=s2f[:, 0:512], scalar=1.0, in1=s2f[:, 0:512],
                op0=ALU.mult, op1=ALU.mult, accum_out=stt_[:, 1:2]))
            b.do("vector", lambda E: E.scalar_tensor_tensor(
                out=junk[:, :], in0=s2f[:, 512:1024], scalar=1.0, in1=s2f[:, 512:1024],
                op0=ALU.mult, op1=ALU.mult, accum_out=stt_[:, 2:3]), inc=(sSt4, 1))
            b.wait("vector", sSt4, sSt4.n)
            emit_ln_tail(sVr4, sRS4)
            if ts_ >= 2:
                b.wait("vector", sOut[r], sOut[r].n)
            b.do("vector", lambda E, r=r: E.tensor_scalar(
                out=outst[:, r, :], in0=s2f[:, :], scalar1=stt_[:, 3:4],
                scalar2=stt_[:, 7:8], op0=ALU.subtract, op1=ALU.mult), inc=(sNM4, 1))
            b.wait("sync", sNM4, sNM4.n)
            b.dma("sync", sOut[r], y_d[ts_ * 128:(ts_ + 1) * 128, :], outst[:, r, :])

    b.wait("sync", sOut[0])
    b.wait("sync", sOut[1])
    p4c.close()
    _finish(nc, b, ctx)


def _finish(nc, b, ctx):
    with nc.Block() as blk:
        @blk.sync
        def _(E):
            for fn in b.stream["sync"]:
                fn(E)

        @blk.tensor
        def _(E):
            for fn in b.stream["tensor"]:
                fn(E)

        @blk.vector
        def _(E):
            for fn in b.stream["vector"]:
                fn(E)

        @blk.scalar
        def _(E):
            for fn in b.stream["scalar"]:
                fn(E)

        @blk.gpsimd
        def _(E):
            for fn in b.stream["gpsimd"]:
                fn(E)
    ctx.close()


def kernel(L0_attn_x, L0_attn_Wq, L0_attn_Wk, L0_attn_Wv, L0_attn_Wo, L0_attn_bo,
           L0_ffn_Wr, L0_ffn_br, L0_ffn_W1, L0_ffn_b1, L0_ffn_W2, L0_ffn_b2):
    x = np.ascontiguousarray(np.asarray(L0_attn_x, dtype=np.float32))
    com = {
        "wq": np.ascontiguousarray(np.asarray(L0_attn_Wq, np.float32)),
        "wk": np.ascontiguousarray(np.asarray(L0_attn_Wk, np.float32)),
        "wv": np.ascontiguousarray(np.asarray(L0_attn_Wv, np.float32)),
        "wo": np.ascontiguousarray(np.asarray(L0_attn_Wo, np.float32)),
        "bo": np.ascontiguousarray(np.asarray(L0_attn_bo, np.float32).reshape(1, D)),
        "wr": np.ascontiguousarray(np.asarray(L0_ffn_Wr, np.float32)),
        "br": np.ascontiguousarray(np.asarray(L0_ffn_br, np.float32).reshape(NE, 1)),
        "w1": np.ascontiguousarray(np.asarray(L0_ffn_W1, np.float32)),
        "b1": np.ascontiguousarray(np.asarray(L0_ffn_b1, np.float32)),
        "w2": np.ascontiguousarray(np.asarray(L0_ffn_W2, np.float32)),
        "b2": np.ascontiguousarray(np.asarray(L0_ffn_b2, np.float32)),
    }
    nc = bass.Bass("TRN2", target_bir_lowering=False)
    build_program(nc)
    in_maps = [dict(com, x=x[i]) for i in range(N_CORES)]
    res = run_bass_kernel_spmd(nc, in_maps, core_ids=list(range(N_CORES)))
    return np.stack([np.asarray(res.results[i]["y"]) for i in range(N_CORES)], axis=0)
